# version 10
# speedup vs baseline: 1.1966x; 1.0000x over previous
import numpy as np
import ml_dtypes
from contextlib import ExitStack

import concourse.bass as bass
import concourse.mybir as mybir
from concourse.bass_utils import run_bass_kernel_spmd

F32 = mybir.dt.float32
BF16 = mybir.dt.bfloat16
Alu = mybir.AluOpType
Act = mybir.ActivationFunctionType
AX = mybir.AxisListType

BF = ml_dtypes.bfloat16

B, S, E = 4, 4096, 1024
H, DH = 16, 64
NPAIR = 8
TOWN = 2048
NT = 16
NTILE = 32
NEXP = 16
QLEN = 4064
EPS = 1e-5


class Sem:
    def __init__(self, ctx, nc, name):
        self.h = ctx.enter_context(nc.semaphore(name))
        self.n = 0

    def inc(self, instr, k=1):
        instr.then_inc(self.h, k)
        self.n += k
        return self.n

    def bump(self, eng, k=1):
        eng.drain().then_inc(self.h, k)
        self.n += k
        return self.n


def _r2(ap2d, sel):
    return (ap2d.rearrange("p (n two) -> p n two", two=2)[:, :, sel:sel + 1]
            .rearrange("p n one -> p (n one)"))


def build_nc(dbg=(), stop_after=None):
    nc = bass.Bass("TRN2", target_bir_lowering=False,
                   detect_race_conditions=False)

    dx = nc.dram_tensor("x", [S, E], F32, kind="ExternalInput").ap()
    dwqkv = nc.dram_tensor("wqkv", [NPAIR, E, 384], F32, kind="ExternalInput").ap()
    dbqkv = nc.dram_tensor("bqkv", [128, 24], F32, kind="ExternalInput").ap()
    dwproj = nc.dram_tensor("wproj", [E, E], F32, kind="ExternalInput").ap()
    dbproj = nc.dram_tensor("bproj", [128, E], F32, kind="ExternalInput").ap()
    dwg = nc.dram_tensor("wg", [E, NEXP], F32, kind="ExternalInput").ap()
    dbg_b = nc.dram_tensor("bg", [128, NEXP], F32, kind="ExternalInput").ap()
    dw1 = nc.dram_tensor("w1s", [NEXP, 8, E, 256], F32, kind="ExternalInput").ap()
    db1 = nc.dram_tensor("b1c", [128, 256], F32, kind="ExternalInput").ap()
    dw2 = nc.dram_tensor("w2", [NEXP, E, E], F32, kind="ExternalInput").ap()
    db2 = nc.dram_tensor("b2", [NEXP, E], F32, kind="ExternalInput").ap()
    dnsc = nc.dram_tensor("nsc", [128, 8], F32, kind="ExternalInput").ap()
    didn = nc.dram_tensor("idn", [128, 128], BF16, kind="ExternalInput").ap()
    didnf = nc.dram_tensor("idnf", [128, 128], F32, kind="ExternalInput").ap()
    dm1 = nc.dram_tensor("mask1", [128, 128], BF16, kind="ExternalInput").ap()
    dm2 = nc.dram_tensor("mask2", [128, 128], BF16, kind="ExternalInput").ap()
    dones = nc.dram_tensor("onesr", [1, 128], BF16, kind="ExternalInput").ap()
    donesf = nc.dram_tensor("onesf", [1, 128], F32, kind="ExternalInput").ap()
    dy = nc.dram_tensor("y", [TOWN, E], F32, kind="ExternalOutput").ap()
    ddbg = {}
    for name, shape, dt_ in dbg:
        ddbg[name] = nc.dram_tensor(name, shape, dt_, kind="ExternalOutput").ap()

    ctx = ExitStack()
    with ctx:
        def sbL(name, shape, dt_):
            return ctx.enter_context(nc.sbuf_tensor("sb_" + name, shape, dt_, side="left"))

        def psb(name, shape, dt_):
            return ctx.enter_context(nc.psum_tensor(name, shape, dt_))

        pT1 = psb("pT1", [128, 1024], BF16)
        pT2 = psb("pT2", [128, 1024], BF16)
        pA = psb("pA", [128, 512], F32)
        pB = psb("pB", [128, 512], F32)
        pC = psb("pC", [128, 512], F32)
        pD = psb("pD", [128, 512], F32)
        pY0 = psb("pY0", [128, 512], F32)
        pY1 = psb("pY1", [128, 512], F32)

        idn = sbL("idn", [128, 128], BF16)
        idnf = sbL("idnf", [128, 128], F32)
        msk1 = sbL("msk1", [128, 128], BF16)
        msk2 = sbL("msk2", [128, 128], BF16)
        onesr = sbL("onesr", [1, 128], BF16)
        onesf = sbL("onesf", [1, 128], F32)
        bqkv = sbL("bqkv", [128, 24], F32)
        bproj = sbL("bprj", [128, E], F32)
        bgb = sbL("bgb", [128, NEXP], F32)
        b1c = sbL("b1c", [128, 256], F32)
        b1c7 = sbL("b1c7", [128, 128], F32)
        nsc = sbL("nsc", [128, 8], F32)
        wgbf = sbL("wgbf", [128, 128], BF16)
        b2bf = sbL("b2bf", [NEXP, E], BF16)
        gfm = sbL("gfm", [NEXP, TOWN], BF16)
        cns = sbL("cns", [128, 2], F32)
        lnr = sbL("lnr", [128, NT], F32)
        lnm = sbL("lnm", [128, NT], F32)
        xnffn = sbL("xnffn", [128, 8 * TOWN], BF16)

        xnffn3 = xnffn[:].rearrange("p (c n) -> p c n", c=8)

        def sm(name):
            return Sem(ctx, nc, name)

        s_c, s_wq = sm("s_c"), sm("s_wq")
        s_cg, s_cw = sm("s_cg"), sm("s_cw")
        s_xr = [sm(f"s_xr{i}") for i in range(3)]
        s_x5 = [sm(f"s_x5{i}") for i in range(2)]
        s_st5 = [sm(f"s_st5{i}") for i in range(2)]
        s_w1r = [sm(f"s_w1r{i}") for i in range(4)]
        s_w2r = [sm(f"s_w2r{i}") for i in range(2)]
        s_mm, s_sc, s_ve, s_gp, s_st, s_tile = (
            sm("s_mm"), sm("s_sc"), sm("s_ve"), sm("s_gp"), sm("s_st"), sm("s_tile"))
        bsem = {k: sm("s_b" + k) for k in ["T1", "T2", "A", "B", "C", "D", "Y0", "Y1"]}
        bank = {"T1": pT1, "T2": pT2, "A": pA, "B": pB, "C": pC, "D": pD,
                "Y0": pY0, "Y1": pY1}
        brel = {k: None for k in bank}

        block = ctx.enter_context(nc.Block())

        def t_guard(tensor, k):
            if brel[k] is not None:
                tensor.wait_ge(brel[k][0], brel[k][1])

        def b_done(instr, k):
            bsem[k].inc(instr)
            brel[k] = (bsem[k].h, bsem[k].n)

        def b_rel_at(k, sem, n):
            brel[k] = (sem.h, n)

        @block.sync
        def _(sync):
            for dst, src in [(idn[:], didn), (idnf[:], didnf), (msk1[:], dm1),
                             (msk2[:], dm2), (onesr[:], dones), (onesf[:], donesf), (bqkv[:], dbqkv),
                             (bproj[:], dbproj), (bgb[:], dbg_b), (b1c[:], db1),
                             (nsc[:], dnsc)]:
                s_c.inc(sync.dma_start(out=dst, in_=src), 16)

        @block.gpsimd
        def _(gpsimd):
            s_cg.inc(gpsimd.dma_start(
                out=wgbf[:].rearrange("p (c n) -> p c n", c=8),
                in_=dwg.rearrange("(c p) n -> p c n", p=128)), 16)
            s_cg.inc(gpsimd.dma_start(out=b2bf[:], in_=db2), 16)

        n_const = s_c.n
        n_cg1 = s_cg.n

        @block.vector
        def _(vector):
            vector.memset(cns[:, 0:1], EPS)
            vector.wait_ge(s_c.h, n_const)
            s_ve.inc(vector.tensor_scalar(b1c7[:], b1c[:, 128:256],
                                          7.0, None, Alu.add))

        scY = ExitStack()
        if stop_after != "p1":
            ycat = scY.enter_context(nc.sbuf_tensor("sb_ycat", [128, 8 * TOWN], BF16, side="left"))
            wproj = scY.enter_context(nc.sbuf_tensor("sb_wprj", [128, 8 * 1024], BF16, side="left"))

            @block.gpsimd
            def _(gpsimd):
                s_cw.inc(gpsimd.dma_start(
                    out=wproj[:].rearrange("p (c n) -> p c n", c=8),
                    in_=dwproj.rearrange("(c p) n -> p c n", p=128)), 16)
        n_const2 = s_cw.n

        with ExitStack() as scX:
            xn_fm = scX.enter_context(
                nc.sbuf_tensor("sb_xn_fm", [128, 8 * S], BF16, side="left"))
            xn3 = xn_fm[:].rearrange("p (c n) -> p c n", c=8)

            with ExitStack() as p1:
                def e1(name, shape, dt_):
                    return p1.enter_context(nc.sbuf_tensor(name, shape, dt_, side="left"))

                xt = [e1(f"xt{i}", [128, 1024], F32) for i in range(3)]
                xn_tm = [e1(f"xntm{i}", [128, 1024], F32) for i in range(2)]
                xn_bf = [e1(f"xnbf{i}", [128, 1024], BF16) for i in range(2)]
                xnf_bf = [e1(f"xnfbf{i}", [128, 1024], BF16) for i in range(2)]
                sqo = e1("sqo", [128, 1024], F32)
                st1 = e1("st1", [128, 24], F32)

                for tau in range(NTILE):
                    po = 12 * (tau % 2)
                    su, sq, mean, m2, var = (st1[:, po + i:po + i + 1]
                                             for i in range(5))
                    lnv, rstd, nmr, rq, rlnv, rrstd = (
                        st1[:, po + i:po + i + 1] for i in range(5, 11))
                    own = tau < NT
                    xb = xt[tau % 3]
                    a = xn_tm[tau % 2]
                    ab = xn_bf[tau % 2]
                    fb = xnf_bf[tau % 2]

                    sx = s_xr[tau % 3]

                    @block.sync
                    def _(sync):
                        if tau >= 3:
                            sync.wait_ge(s_tile.h, tau - 2)
                        sx.inc(sync.dma_start(
                            out=xb[:], in_=dx[tau * 128:(tau + 1) * 128, :]), 16)
                    nx = sx.n

                    @block.vector
                    def _(vector):
                        vector.wait_ge(sx.h, nx)
                        vector.reduce_sum(su, xb[:], AX.X)
                        s_ve.bump(vector)
                    nv1 = s_ve.n

                    @block.scalar
                    def _(scalar):
                        scalar.wait_ge(sx.h, nx)
                        scalar.wait_ge(s_c.h, n_const)
                        scalar.activation(sqo[:], xb[:], Act.Square, accum_out=sq)
                        s_sc.bump(scalar)
                    ns1 = s_sc.n

                    @block.vector
                    def _(vector):
                        vector.wait_ge(s_sc.h, ns1)
                        vector.wait_ge(s_ve.h, nv1)
                        vector.tensor_scalar(mean, su, 1.0 / 1024, None, Alu.mult)
                        vector.drain()
                        vector.tensor_tensor(m2, mean, mean, Alu.mult)
                        vector.drain()
                        vector.scalar_tensor_tensor(
                            var, sq, 1.0 / 1024, m2, Alu.mult, Alu.subtract)
                        s_ve.bump(vector)
                    nv2 = s_ve.n

                    @block.scalar
                    def _(scalar):
                        scalar.wait_ge(s_ve.h, nv2)
                        scalar.activation(lnv, var, Act.Ln, bias=cns[:, 0:1])
                        scalar.drain()
                        scalar.activation(rstd, lnv, Act.Exp, scale=-0.5)
                        s_sc.bump(scalar)
                    ns2 = s_sc.n

                    @block.vector
                    def _(vector):
                        vector.wait_ge(s_sc.h, ns2)
                        vector.scalar_tensor_tensor(
                            nmr, mean, -1.0, rstd, Alu.mult, Alu.mult)
                        if own:
                            vector.tensor_copy(lnr[:, tau:tau + 1], rstd)
                            vector.drain()
                            vector.tensor_copy(lnm[:, tau:tau + 1], nmr)
                        s_ve.bump(vector)
                    nv3 = s_ve.n

                    @block.scalar
                    def _(scalar):
                        scalar.wait_ge(s_ve.h, nv3)
                        s_sc.inc(scalar.activation(a[:], xb[:], Act.Identity,
                                                   scale=rstd, bias=nmr))
                        s_tile.bump(scalar)
                    ns3 = s_sc.n

                    ns4 = s_sc.n
                    if own:
                        @block.scalar
                        def _(scalar):
                            scalar.activation(sqo[:], a[:], Act.Square, accum_out=rq)
                            scalar.drain()
                            scalar.activation(rlnv, rq, Act.Ln, scale=1.0 / 1024,
                                              bias=cns[:, 0:1])
                            scalar.drain()
                            scalar.activation(rrstd, rlnv, Act.Exp, scale=-0.5)
                            scalar.drain()
                            s_sc.inc(scalar.activation(fb[:], a[:], Act.Identity,
                                                       scale=rrstd))
                        ns4 = s_sc.n

                    @block.vector
                    def _(vector):
                        vector.wait_ge(s_sc.h, ns3)
                        s_ve.inc(vector.tensor_copy(ab[:], a[:]))
                    nv4 = s_ve.n

                    for c in range(8):
                        tb = "T1" if (tau * 8 + c) % 2 == 0 else "T2"

                        @block.tensor
                        def _(tensor):
                            tensor.wait_ge(s_ve.h, nv4)
                            if own:
                                tensor.wait_ge(s_sc.h, ns4)
                            t_guard(tensor, tb)
                            s_mm.inc(tensor.transpose(
                                bank[tb][:, 0:128], ab[:, c * 128:(c + 1) * 128],
                                idn[:]))
                            if own:
                                s_mm.inc(tensor.transpose(
                                    bank[tb][:, 128:256], fb[:, c * 128:(c + 1) * 128],
                                    idn[:]))
                        nm = s_mm.n

                        loc = tau if own else tau - NT
                        dst = _r2(xn3[:, c, loc * 256:(loc + 1) * 256], 0 if own else 1)

                        @block.vector
                        def _(vector):
                            vector.wait_ge(s_mm.h, nm)
                            v = vector.tensor_copy(dst, bank[tb][:, 0:128])
                            if own:
                                v = vector.tensor_copy(
                                    xnffn3[:, c, tau * 128:(tau + 1) * 128],
                                    bank[tb][:, 128:256])
                            b_done(v, tb)
                            s_ve.bump(vector)

                @block.vector
                def _(vector):
                    vector.wait_ge(s_ve.h, s_ve.n)
                    for c in range(8):
                        s_ve.inc(vector.tensor_scalar(
                            xnffn[:, c * TOWN:(c + 1) * TOWN],
                            xnffn[:, c * TOWN:(c + 1) * TOWN],
                            nsc[:, c:c + 1], None, Alu.mult))
                nsc_done = s_ve.n

                lg = e1("lg", [128, 16], F32)
                r4 = e1("r4", [128, 16], F32)
                mx4 = e1("mx4", [128, 4], F32)
                en = e1("en", [128, 16], F32)
                gtm = e1("gtm", [128, 16], BF16)
                for ts in range(NT):
                    @block.tensor
                    def _(tensor):
                        tensor.wait_ge(s_ve.h, nsc_done)
                        if ts == 0:
                            tensor.wait_ge(s_cg.h, n_cg1)
                        t_guard(tensor, "A")
                        for c in range(8):
                            s_mm.inc(tensor.matmul(
                                pA[:, 0:16], xnffn3[:, c, ts * 128:(ts + 1) * 128],
                                wgbf[:, c * 16:(c + 1) * 16],
                                start=(c == 0), stop=(c == 7)))
                    nm = s_mm.n

                    @block.vector
                    def _(vector):
                        vector.wait_ge(s_mm.h, nm)
                        v = vector.tensor_tensor(lg[:], pA[:, 0:16], bgb[:], Alu.add)
                        b_done(v, "A")
                        vector.drain()
                        vector.tensor_copy(r4[:], lg[:])
                        for i in range(4):
                            vector.drain()
                            vector.tensor_reduce(mx4[:, i:i + 1], r4[:], AX.X, Alu.max)
                            if i < 3:
                                vector.drain()
                                vector.tensor_single_scalar(en[:], r4[:],
                                                            mx4[:, i:i + 1], Alu.is_ge)
                                vector.drain()
                                vector.scalar_tensor_tensor(r4[:], en[:], -1e30, r4[:],
                                                            Alu.mult, Alu.add)
                        vector.drain()
                        vector.tensor_single_scalar(
                            en[:], lg[:], mx4[:, 3:4], Alu.is_ge)
                        s_ve.bump(vector)
                    nv = s_ve.n

                    @block.scalar
                    def _(scalar):
                        scalar.wait_ge(s_ve.h, nv)
                        scalar.activation(r4[:], lg[:], Act.Exp)
                        s_sc.bump(scalar)
                    nssc = s_sc.n

                    @block.vector
                    def _(vector):
                        vector.wait_ge(s_sc.h, nssc)
                        vector.tensor_tensor(en[:], en[:], r4[:], Alu.mult)
                        vector.drain()
                        vector.tensor_reduce(mx4[:, 0:1], en[:], AX.X, Alu.add)
                        vector.drain()
                        vector.reciprocal(mx4[:, 1:2], mx4[:, 0:1])
                        vector.drain()
                        vector.tensor_single_scalar(
                            gtm[:], en[:], mx4[:, 1:2], Alu.mult)
                        s_ve.bump(vector)
                    nv5 = s_ve.n

                    @block.tensor
                    def _(tensor):
                        tensor.wait_ge(s_ve.h, nv5)
                        t_guard(tensor, "T1")
                        s_mm.inc(tensor.transpose(pT1[0:16, 0:128], gtm[:], idn[:]))
                    nm2 = s_mm.n

                    @block.vector
                    def _(vector):
                        vector.wait_ge(s_mm.h, nm2)
                        v = vector.tensor_copy(gfm[:, ts * 128:(ts + 1) * 128],
                                               pT1[0:16, 0:128])
                        b_done(v, "T1")
                        s_ve.bump(vector)

            p1_done = s_ve.n
            if stop_after == "p1":
                @block.gpsimd
                def _(gpsimd):
                    gpsimd.wait_ge(s_ve.h, s_ve.n)
                    s_gp.inc(gpsimd.dma_start(out=dy[0:128, :],
                                              in_=xn_fm[:, 0:1024]), 16)
                    gpsimd.wait_ge(s_gp.h, s_gp.n)
                return nc

            with ExitStack() as p2:
                def e2(name, shape, dt_):
                    return p2.enter_context(nc.sbuf_tensor(name, shape, dt_, side="left"))

                wq_bf = e2("wq_bf", [128, 8 * 384], BF16)
                k_bf = e2("k_bf", [128, S], BF16)
                scr8 = e2("scr8", [128, S], BF16)
                vtm = e2("vtm", [128, S], BF16)
                v1h = e2("sb_v1h", [128, 32 * 66], BF16)
                v1g = e2("sb_v1g", [128, 32 * 66], BF16)
                q_bf = e2("q_bf", [128, TOWN], BF16)
                pbuf = [e2(f"pb{i}", [128, 512], BF16) for i in range(4)]
                ytm = [e2(f"ytm{i}", [128, 64], BF16) for i in range(4)]
                mbf = e2("mbf", [128, 1], BF16)
                qs = e2("qs", [128, 512], F32)

                wq3 = wq_bf[:].rearrange("p (c n) -> p c n", c=8)
                v1h3 = v1h[:].rearrange("p (t c) -> p t c", c=66)
                v1g3 = v1g[:].rearrange("p (t c) -> p t c", c=66)
                vtm3 = vtm[:].rearrange("p (t c) -> p t c", c=128)

                QMN, QMX, QTH1, QTH2, QTH3, QSS, QMN2, QH2 = 0, 32, 64, 96, 128, 160, 192, 224
                KV, KSC, MABS, CSH = 448, 456, 472, 474

                def bc3(col, n_t, rows=128):
                    one = (qs[0:rows, col:col + n_t]
                           .rearrange("p (n one) -> p n one", one=1))
                    return one.broadcast_to((rows, n_t, 64))

                @block.vector
                def _(vector):
                    vector.memset(qs[:], 0.0)

                pair_qproj_done = [0] * NPAIR
                trans_done = 0
                trans_hist = [0, 0]
                prev_scr8_free = 0

                sub = None
                if stop_after and stop_after.startswith("p2:"):
                    sub = stop_after.split(":")[1]
                npair_emit = 1 if sub else NPAIR
                for p in range(npair_emit):
                    @block.gpsimd
                    def _(gpsimd):
                        if p > 0:
                            gpsimd.wait_ge(s_mm.h, pair_qproj_done[p - 1])
                        else:
                            gpsimd.wait_ge(s_ve.h, p1_done)
                        s_wq.inc(gpsimd.dma_start(
                            out=wq3, in_=dwqkv[p].rearrange("(c p) n -> p c n", p=128)),
                            16)
                    nwq = s_wq.n

                    for g in range(8):
                        kb = "AB"[g % 2]

                        @block.tensor
                        def _(tensor):
                            tensor.wait_ge(s_wq.h, nwq)
                            if p == 0 and g == 0:
                                tensor.wait_ge(s_ve.h, p1_done)
                            t_guard(tensor, kb)
                            for c in range(8):
                                s_mm.inc(tensor.matmul(
                                    bank[kb][:], wq3[:, c, 128:256],
                                    xn3[:, c, g * 512:(g + 1) * 512],
                                    start=(c == 0), stop=(c == 7)))
                        nm = s_mm.n

                        @block.scalar
                        def _(scalar):
                            scalar.wait_ge(s_mm.h, nm)
                            if p > 0 and g == 0:
                                scalar.wait_ge(s_ve.h, prev_scr8_free)
                            v = scalar.activation(
                                scr8[:, g * 512:(g + 1) * 512], bank[kb][:], Act.Identity,
                                bias=bqkv[:, 3 * p + 1:3 * p + 2])
                            s_sc.inc(v)
                            b_rel_at(kb, s_sc, s_sc.n)
                    k_evac = s_sc.n
                    if sub == 'K': continue

                    @block.vector
                    def _(vector):
                        vector.wait_ge(s_sc.h, k_evac)
                        vector.tensor_reduce(qs[:, KV:KV + 1], scr8[:, 0:QLEN],
                                             AX.X, Alu.min)
                        vector.tensor_reduce(qs[:, KV + 1:KV + 2],
                                             scr8[:, 0:QLEN], AX.X, Alu.max)
                        s_ve.bump(vector)
                    nv = s_ve.n

                    @block.tensor
                    def _(tensor):
                        tensor.wait_ge(s_ve.h, nv)
                        t_guard(tensor, "C")
                        t_guard(tensor, "D")
                        tensor.matmul(pC[0:1, 0:128], qs[:, KV:KV + 1], idnf[:],
                                      start=True, stop=True)
                        s_mm.inc(tensor.matmul(pD[0:1, 0:128],
                                               qs[:, KV + 1:KV + 2], idnf[:],
                                               start=True, stop=True))
                    nm = s_mm.n

                    CH = 480

                    @block.vector
                    def _(vector):
                        vector.wait_ge(s_mm.h, nm)
                        vector.tensor_reduce(qs[0:1, CH:CH + 1], pC[0:1, 0:64],
                                             AX.X, Alu.min)
                        vector.tensor_reduce(qs[0:1, CH + 1:CH + 2], pC[0:1, 64:128],
                                             AX.X, Alu.min)
                        v0 = vector.tensor_reduce(qs[0:1, CH + 2:CH + 3],
                                                  pD[0:1, 0:64], AX.X, Alu.max)
                        b_done(v0, "C")
                        v = vector.tensor_reduce(qs[0:1, CH + 3:CH + 4],
                                                 pD[0:1, 64:128], AX.X, Alu.max)
                        b_done(v, "D")
                        mn2 = qs[0:1, KSC:KSC + 2]
                        mx2 = qs[0:1, CH + 8:CH + 10]
                        vector.drain()
                        vector.tensor_scalar(qs[0:1, CH + 4:CH + 6],
                                             qs[0:1, CH:CH + 2], 1e4, None, Alu.min)
                        vector.drain()
                        vector.tensor_scalar(mn2, qs[0:1, CH + 4:CH + 6],
                                             -1e4, None, Alu.max)
                        vector.tensor_scalar(qs[0:1, CH + 6:CH + 8],
                                             qs[0:1, CH + 2:CH + 4], 1e4, None, Alu.min)
                        vector.drain()
                        vector.tensor_scalar(mx2, qs[0:1, CH + 6:CH + 8],
                                             -1e4, None, Alu.max)
                        vector.drain()
                        vector.tensor_tensor(qs[0:1, CH + 10:CH + 12], mx2, mn2,
                                             Alu.subtract)
                        vector.drain()
                        vector.tensor_scalar(qs[0:1, CH + 12:CH + 14],
                                             qs[0:1, CH + 10:CH + 12],
                                             1.0 / 3.0, None, Alu.mult)
                        sK = qs[0:1, KSC + 4:KSC + 6]
                        vector.drain()
                        vector.tensor_scalar(sK, qs[0:1, CH + 12:CH + 14],
                                             1e-8, None, Alu.max)
                        vector.drain()
                        for i, f in enumerate([0.5, 1.5, 2.5]):
                            th = qs[0:1, KSC + 6 + 2 * i:KSC + 8 + 2 * i]
                            vector.scalar_tensor_tensor(th, sK, f, mn2,
                                                        Alu.mult, Alu.add)
                        s_ve.bump(vector)
                    nv2 = s_ve.n

                    @block.tensor
                    def _(tensor):
                        tensor.wait_ge(s_ve.h, nv2)
                        t_guard(tensor, "C")
                        s_mm.inc(tensor.matmul(pC[:, 0:12], onesf[:],
                                               qs[0:1, KSC:KSC + 12],
                                               start=True, stop=True))
                    nbc = s_mm.n

                    @block.vector
                    def _(vector):
                        vector.wait_ge(s_mm.h, nbc)
                        p3v = pC[:, 0:12].rearrange("p (n two) -> p n two", two=2)
                        vector.tensor_copy(
                            qs[0:64, KV:KV + 3],
                            p3v[0:64, 3:6, 0:1].rearrange("p n one -> p (n one)"))
                        vector.tensor_copy(
                            qs[64:128, KV:KV + 3],
                            p3v[64:128, 3:6, 1:2].rearrange("p n one -> p (n one)"))
                        vector.tensor_copy(qs[0:64, KV + 3:KV + 4], pC[0:64, 4:5])
                        vector.tensor_copy(qs[64:128, KV + 3:KV + 4], pC[64:128, 5:6])
                        vector.tensor_copy(qs[0:64, KV + 4:KV + 5], pC[0:64, 0:1])
                        vector.tensor_copy(qs[64:128, KV + 4:KV + 5],
                                            pC[64:128, 1:2])
                        b_done(bsem["C"], "C") if False else None
                        vector.drain()
                        v = vector.tensor_copy(qs[0:1, 508:509], qs[0:1, 0:1])
                        b_done(v, "C")
                        s_ve.bump(vector)
                    ng = s_ve.n

                    @block.vector
                    def _(vector):
                        vector.wait_ge(s_ve.h, ng)
                        vector.tensor_single_scalar(k_bf[:, 0:QLEN], scr8[:, 0:QLEN],
                                                    qs[:, KV:KV + 1], Alu.is_ge)
                        vector.scalar_tensor_tensor(k_bf[:, 0:QLEN], scr8[:, 0:QLEN],
                                                    qs[:, KV + 1:KV + 2],
                                                    k_bf[:, 0:QLEN], Alu.is_ge, Alu.add)
                        vector.scalar_tensor_tensor(
                            k_bf[:, 0:QLEN], scr8[:, 0:QLEN], qs[:, KV + 2:KV + 3],
                            k_bf[:, 0:QLEN], Alu.is_ge, Alu.add)
                        vector.tensor_scalar(k_bf[:, 0:QLEN], k_bf[:, 0:QLEN],
                                             qs[:, KV + 3:KV + 4],
                                             qs[:, KV + 4:KV + 5], Alu.mult, Alu.add)
                        s_ve.inc(vector.tensor_copy(k_bf[:, QLEN:S], scr8[:, QLEN:S]))
                    k_ready = s_ve.n
                    if sub == 'Kq': continue

                    if p == 0 and "kbf0" in ddbg:
                        @block.sync
                        def _(sync):
                            sync.wait_ge(s_ve.h, k_ready)
                            s_st.inc(sync.dma_start(out=ddbg["kbf0"], in_=k_bf[:]), 16)
                            s_st.inc(sync.dma_start(out=ddbg["qs0"], in_=qs[:]), 16)

                    @block.vector
                    def _(vector):
                        vector.reduce_max(qs[:, MABS:MABS + 1], k_bf[:], AX.X,
                                          apply_absolute_value=True)
                        vector.drain()
                        vector.tensor_copy(mbf[:], qs[:, MABS:MABS + 1])
                        s_ve.bump(vector)
                    nMb = s_ve.n

                    for g in range(8):
                        vb = "AB"[g % 2]

                        @block.tensor
                        def _(tensor):
                            t_guard(tensor, vb)
                            for c in range(8):
                                s_mm.inc(tensor.matmul(
                                    bank[vb][:], wq3[:, c, 256:384],
                                    xn3[:, c, g * 512:(g + 1) * 512],
                                    start=(c == 0), stop=(c == 7)))
                        nm = s_mm.n

                        @block.scalar
                        def _(scalar):
                            scalar.wait_ge(s_mm.h, nm)
                            if g == 0:
                                scalar.wait_ge(s_ve.h, k_ready)
                            v = scalar.activation(
                                scr8[:, g * 512:(g + 1) * 512], bank[vb][:], Act.Identity,
                                bias=bqkv[:, 3 * p + 2:3 * p + 3])
                            s_sc.inc(v)
                            b_rel_at(vb, s_sc, s_sc.n)
                    v_evac = s_sc.n
                    if sub == 'V': continue

                    for t in range(32):
                        tb = "T1" if t % 2 == 0 else "T2"

                        @block.tensor
                        def _(tensor):
                            if t == 0:
                                tensor.wait_ge(s_sc.h, v_evac)
                            t_guard(tensor, tb)
                            s_mm.inc(tensor.transpose(
                                bank[tb][:, 0:128], scr8[:, t * 128:(t + 1) * 128],
                                idn[:]))
                        nm = s_mm.n

                        @block.vector
                        def _(vector):
                            vector.wait_ge(s_mm.h, nm)
                            v = vector.tensor_copy(vtm[:, t * 128:(t + 1) * 128],
                                                   bank[tb][:, 0:128])
                            b_done(v, tb)
                            s_ve.bump(vector)
                    v_tm_done = s_ve.n
                    if sub == 'Vt': continue

                    @block.vector
                    def _(vector):
                        vector.wait_ge(s_ve.h, v_tm_done)
                        for hh, qb in [(0, 0), (64, QH2)]:
                            src31 = vtm3[:, 0:31, hh:hh + 64]
                            vector.tensor_reduce(qs[:, qb + QMN:qb + QMN + 31], src31,
                                                 AX.X, Alu.min)
                            vector.tensor_reduce(qs[:, qb + QMX:qb + QMX + 31], src31,
                                                 AX.X, Alu.max)
                            s31 = vtm3[0:96, 31:32, hh:hh + 64]
                            vector.tensor_reduce(qs[0:96, qb + QMN + 31:qb + QMN + 32],
                                                 s31, AX.X, Alu.min)
                            vector.tensor_reduce(qs[0:96, qb + QMX + 31:qb + QMX + 32],
                                                 s31, AX.X, Alu.max)
                            mn = qs[:, qb + QMN:qb + QMN + 32]
                            mx = qs[:, qb + QMX:qb + QMX + 32]
                            vector.drain()
                            vector.tensor_scalar(qs[:, qb + QMN2:qb + QMN2 + 32],
                                                 mn, 1e4, None, Alu.min)
                            vector.drain()
                            vector.tensor_scalar(mn, qs[:, qb + QMN2:qb + QMN2 + 32],
                                                 -1e4, None, Alu.max)
                            vector.tensor_scalar(qs[:, qb + QSS:qb + QSS + 32],
                                                 mx, 1e4, None, Alu.min)
                            vector.drain()
                            vector.tensor_scalar(mx, qs[:, qb + QSS:qb + QSS + 32],
                                                 -1e4, None, Alu.max)
                            vector.drain()
                            sV = qs[:, qb + QSS:qb + QSS + 32]
                            vector.tensor_tensor(sV, mx, mn, Alu.subtract)
                            vector.drain()
                            vector.tensor_scalar(qs[:, qb + QTH1:qb + QTH1 + 32],
                                                 sV, 1.0 / 3.0, None, Alu.mult)
                            vector.drain()
                            vector.tensor_scalar(sV, qs[:, qb + QTH1:qb + QTH1 + 32],
                                                 1e-8, None, Alu.max)
                            vector.drain()
                            for i, f in enumerate([0.5, 1.5, 2.5]):
                                th = qs[:, qb + QTH1 + 32 * i:qb + QTH1 + 32 * i + 32]
                                vector.scalar_tensor_tensor(th, sV, f, mn,
                                                            Alu.mult, Alu.add)
                            vector.drain()
                            vector.tensor_copy(qs[:, qb + QMN2:qb + QMN2 + 32], mn)
                        s_ve.bump(vector)
                    nvq = s_ve.n

                    @block.vector
                    def _(vector):
                        vector.wait_ge(s_ve.h, nvq)
                        vector.memset(v1h[:], 1.0)
                        vector.memset(v1g[:], 1.0)
                        tmp = scr8[:, 0:31 * 64].rearrange("p (t c) -> p t c", c=64)
                        tmp1 = tmp[0:96, 0:1, :]
                        for hh, qb, v3 in [(0, 0, v1h3), (64, QH2, v1g3)]:
                            src = vtm3[:, 0:31, hh:hh + 64]
                            dst = v3[:, 0:31, 0:64]
                            vector.tensor_tensor(dst, src, bc3(QTH1 + qb, 31), Alu.is_ge)
                            vector.tensor_tensor(tmp, src, bc3(QTH2 + qb, 31), Alu.is_ge)
                            vector.tensor_tensor(dst, dst, tmp, Alu.add)
                            vector.tensor_tensor(tmp, src, bc3(QTH3 + qb, 31), Alu.is_ge)
                            vector.tensor_tensor(dst, dst, tmp, Alu.add)
                            vector.tensor_tensor(dst, dst, bc3(QSS + qb, 31), Alu.mult)
                            vector.tensor_tensor(dst, dst, bc3(QMN2 + qb, 31), Alu.add)
                            src1 = vtm3[0:96, 31:32, hh:hh + 64]
                            dst1 = v3[0:96, 31:32, 0:64]
                            vector.tensor_tensor(dst1, src1, bc3(QTH1 + qb + 31, 1, 96),
                                                 Alu.is_ge)
                            vector.tensor_tensor(tmp1, src1, bc3(QTH2 + qb + 31, 1, 96),
                                                 Alu.is_ge)
                            vector.tensor_tensor(dst1, dst1, tmp1, Alu.add)
                            vector.tensor_tensor(tmp1, src1, bc3(QTH3 + qb + 31, 1, 96),
                                                 Alu.is_ge)
                            vector.tensor_tensor(dst1, dst1, tmp1, Alu.add)
                            vector.tensor_tensor(dst1, dst1, bc3(QSS + qb + 31, 1, 96),
                                                 Alu.mult)
                            vector.tensor_tensor(dst1, dst1, bc3(QMN2 + qb + 31, 1, 96),
                                                 Alu.add)
                            vector.tensor_copy(v3[96:128, 31:32, 0:64],
                                               vtm3[96:128, 31:32, hh:hh + 64])
                        s_ve.bump(vector)
                    v1_done = s_ve.n
                    prev_scr8_free = v1_done
                    if sub == 'Vq': continue

                    for g in range(4):
                        qb_ = "CD"[g % 2]

                        @block.tensor
                        def _(tensor):
                            t_guard(tensor, qb_)
                            for c in range(8):
                                rhs = _r2(xn3[:, c, :], 0)[:, g * 512:(g + 1) * 512]
                                s_mm.inc(tensor.matmul(
                                    bank[qb_][:], wq3[:, c, 0:128], rhs,
                                    start=(c == 0), stop=(c == 7)))
                        nm = s_mm.n

                        @block.scalar
                        def _(scalar):
                            scalar.wait_ge(s_mm.h, nm)
                            v = scalar.activation(
                                q_bf[:, g * 512:(g + 1) * 512], bank[qb_][:], Act.Identity,
                                bias=bqkv[:, 3 * p:3 * p + 1])
                            s_sc.inc(v)
                            b_rel_at(qb_, s_sc, s_sc.n)
                    q_done = s_sc.n
                    pair_qproj_done[p] = s_mm.n
                    if sub == 'Q': continue

                    @block.scalar
                    def _(scalar):
                        scalar.wait_ge(s_sc.h, q_done)
                        scalar.wait_ge(s_ve.h, v1_done)
                        s_sc.inc(scalar.activation(vtm[:, 0:TOWN], q_bf[:], Act.Abs))
                    nab = s_sc.n

                    for hi, hh in [(0, 0), (1, 64)]:
                        for g in range(4):
                            bb = "CD"[g % 2]

                            @block.tensor
                            def _(tensor):
                                tensor.wait_ge(s_sc.h, nab)
                                tensor.wait_ge(s_ve.h, nMb)
                                t_guard(tensor, bb)
                                s_mm.inc(tensor.matmul(
                                    bank[bb][0:1, 0:512], mbf[hh:hh + 64, 0:1],
                                    vtm[hh:hh + 64, g * 512:(g + 1) * 512],
                                    start=True, stop=True, tile_position=(hh, 0)))
                            nm = s_mm.n

                            @block.vector
                            def _(vector):
                                vector.wait_ge(s_mm.h, nm)
                                v = vector.tensor_reduce(
                                    qs[0:1, KSC + 8 + 4 * hi + g:KSC + 9 + 4 * hi + g],
                                    bank[bb][0:1, 0:512], AX.X, Alu.max)
                                b_done(v, bb)
                                s_ve.bump(vector)

                    @block.vector
                    def _(vector):
                        vector.wait_ge(s_ve.h, s_ve.n)
                        vector.drain()
                        for hi in range(2):
                            base = KSC + 8 + 4 * hi
                            vector.tensor_reduce(qs[0:1, KSC + hi:KSC + hi + 1],
                                                 qs[0:1, base:base + 4], AX.X, Alu.max)
                        vector.drain()
                        vector.tensor_scalar(qs[0:1, KSC:KSC + 2],
                                             qs[0:1, KSC:KSC + 2],
                                             -0.125, None, Alu.mult)
                        s_ve.bump(vector)
                    nbv = s_ve.n

                    @block.tensor
                    def _(tensor):
                        tensor.wait_ge(s_ve.h, nbv)
                        t_guard(tensor, "C")
                        s_mm.inc(tensor.matmul(pC[:, 16:18], onesf[:],
                                               qs[0:1, KSC:KSC + 2],
                                               start=True, stop=True))
                    nsh1 = s_mm.n

                    @block.vector
                    def _(vector):
                        vector.wait_ge(s_mm.h, nsh1)
                        v = vector.tensor_copy(qs[:, CSH:CSH + 2], pC[:, 16:18])
                        b_done(v, "C")
                        s_ve.bump(vector)
                    nshift = s_ve.n
                    if sub == 'b': continue

                    jobi = 0
                    for qg in range(4):
                        y_guarded = False
                        Tq = 8 * qg + 8

                        def emit_av(job):
                            (t_, lo_, nsub_, sel_, bh_, nb0_, nse_, nm_, nvm_) = job

                            @block.tensor
                            def _(tensor):
                                if nm_:
                                    tensor.wait_ge(s_ve.h, nvm_)
                                else:
                                    tensor.wait_ge(bsem[bh_].h, nb0_)
                                nonlocal y_guarded
                                if not y_guarded:
                                    t_guard(tensor, "Y0")
                                    t_guard(tensor, "Y1")
                                    y_guarded = True
                                pbh_ = pbuf[2 * sel_]
                                pbh2_ = pbuf[2 * sel_ + 1]
                                for si in range(nsub_):
                                    sl = lo_ + si - 4 * qg
                                    first = (t_ == 0 and si == 0)
                                    last = (t_ == Tq - 1)
                                    s_mm.inc(tensor.matmul(
                                        pY0[:, sl * 65:sl * 65 + 65],
                                        pbh_[:, si * 128:(si + 1) * 128],
                                        v1h[:, t_ * 66:t_ * 66 + 65],
                                        start=first, stop=last,
                                        skip_group_check=True))
                                if not nm_:
                                    tensor.wait_ge(s_sc.h, nse_)
                                for si in range(nsub_):
                                    sl = lo_ + si - 4 * qg
                                    first = (t_ == 0 and si == 0)
                                    last = (t_ == Tq - 1)
                                    s_mm.inc(tensor.matmul(
                                        pY1[:, sl * 65:sl * 65 + 65],
                                        pbh2_[:, si * 128:(si + 1) * 128],
                                        v1g[:, t_ * 66:t_ * 66 + 65],
                                        start=first, stop=last,
                                        skip_group_check=True))

                        pend = None
                        for t in range(Tq):
                            lo = max(4 * qg, t // 2)
                            nsub = 4 * qg + 4 - lo
                            w = 128 * nsub
                            qoff = 512 * qg + 128 * (lo - 4 * qg)
                            sel = jobi % 2
                            jobi += 1
                            bh = "AB"[0] if sel == 0 else "CD"[0]
                            bh2 = "AB"[1] if sel == 0 else "CD"[1]
                            pbh, pbh2 = pbuf[2 * sel], pbuf[2 * sel + 1]

                            @block.tensor
                            def _(tensor):
                                if t == 0 and qg == 0:
                                    tensor.wait_ge(s_ve.h, max(v1_done, nshift))
                                t_guard(tensor, bh)
                                t_guard(tensor, bh2)
                                s_mm.inc(tensor.matmul(
                                    bank[bh][:, 0:w],
                                    k_bf[0:64, t * 128:(t + 1) * 128],
                                    q_bf[0:64, qoff:qoff + w],
                                    start=True, stop=True, tile_position=(0, 0)))
                                s_mm.inc(tensor.matmul(
                                    bank[bh2][:, 0:w],
                                    k_bf[64:128, t * 128:(t + 1) * 128],
                                    q_bf[64:128, qoff:qoff + w],
                                    start=True, stop=True, tile_position=(64, 0)))
                            nms = s_mm.n

                            @block.scalar
                            def _(scalar):
                                scalar.wait_ge(s_mm.h, nms)
                                v = scalar.activation(pbh[:, 0:w], bank[bh][:, 0:w],
                                                      Act.Exp, scale=0.125,
                                                      bias=qs[:, CSH:CSH + 1])
                                b_done(v, bh)
                                v2 = scalar.activation(pbh2[:, 0:w], bank[bh2][:, 0:w],
                                                       Act.Exp, scale=0.125,
                                                       bias=qs[:, CSH + 1:CSH + 2])
                                s_sc.inc(v2)
                                b_rel_at(bh2, s_sc, s_sc.n)
                            nse = s_sc.n
                            nb0 = bsem[bh].n

                            md = t // 2
                            need_mask = lo <= md < 4 * qg + 4
                            if need_mask:
                                mc = 128 * (md - lo)
                                mk = msk1 if t % 2 == 0 else msk2

                                @block.vector
                                def _(vector):
                                    vector.wait_ge(s_sc.h, nse)
                                    vector.tensor_tensor(pbh[:, mc:mc + 128],
                                                         pbh[:, mc:mc + 128],
                                                         mk[:], Alu.mult)
                                    s_ve.inc(vector.tensor_tensor(
                                        pbh2[:, mc:mc + 128], pbh2[:, mc:mc + 128],
                                        mk[:], Alu.mult))
                            nvm = s_ve.n

                            if pend is not None:
                                emit_av(pend)
                            pend = (t, lo, nsub, sel, bh, nb0, nse, need_mask, nvm)
                        emit_av(pend)
                        nav = s_mm.n

                        zcol = qs[:, 500:504]
                        for si in range(4):
                            sub = 4 * qg + si

                            zc = qs[:, 480 + 2 * si:482 + 2 * si]
                            yt0 = ytm[(si % 2) * 2]
                            yt1 = ytm[(si % 2) * 2 + 1]

                            @block.vector
                            def _(vector):
                                vector.wait_ge(s_mm.h, nav)
                                vector.reciprocal(zc[:, 0:1],
                                                  pY0[:, si * 65 + 64:si * 65 + 65])
                                vector.reciprocal(
                                    zc[:, 1:2], pY1[:, si * 65 + 64:si * 65 + 65])
                                s_ve.bump(vector)
                            nrc = s_ve.n

                            th2 = trans_hist[-2]

                            @block.scalar
                            def _(scalar):
                                scalar.wait_ge(s_ve.h, nrc)
                                if th2:
                                    scalar.wait_ge(s_mm.h, th2)
                                v = scalar.activation(yt0[:],
                                                      pY0[:, si * 65:si * 65 + 64],
                                                      Act.Identity, scale=zc[:, 0:1])
                                if si == 3:
                                    b_done(v, "Y0")
                                v2 = scalar.activation(yt1[:],
                                                       pY1[:, si * 65:si * 65 + 64],
                                                       Act.Identity, scale=zc[:, 1:2])
                                s_sc.inc(v2)
                                if si == 3:
                                    b_rel_at("Y1", s_sc, s_sc.n)
                            nyt = s_sc.n

                            @block.tensor
                            def _(tensor):
                                tensor.wait_ge(s_sc.h, nyt)
                                t_guard(tensor, "T1")
                                s_mm.inc(tensor.transpose(pT1[0:64, 0:128],
                                                          yt0[:], idn[:]))
                                t_guard(tensor, "T2")
                                s_mm.inc(tensor.transpose(pT2[64:128, 0:128],
                                                          yt1[:], idn[:]))
                            trans_done = s_mm.n
                            trans_hist.append(trans_done)

                            @block.vector
                            def _(vector):
                                vector.wait_ge(s_mm.h, trans_done)
                                dst = ycat[:, p * TOWN + sub * 128:
                                           p * TOWN + (sub + 1) * 128]
                                v = vector.tensor_copy(dst[0:64, :], pT1[0:64, 0:128])
                                b_done(v, "T1")
                                v2 = vector.tensor_copy(dst[64:128, :],
                                                        pT2[64:128, 0:128])
                                b_done(v2, "T2")
                                s_ve.bump(vector)

                attn_done = s_ve.n

                if ddbg:
                    @block.sync
                    def _(sync):
                        sync.wait_ge(s_ve.h, attn_done)
                        for nm_, buf in [("ycat", ycat), ("xn_fm", xn_fm),
                                         ("lnr", lnr), ("lnm", lnm),
                                         ("kbf", k_bf), ("v1", v1h), ("v1g", v1g), ("qbf", q_bf), ("qsd", qs),
                                         ("xnffn", xnffn), ("gfm", gfm)]:
                            if nm_ in ddbg:
                                s_st.inc(sync.dma_start(out=ddbg[nm_], in_=buf[:]), 16)

        if stop_after and stop_after.startswith("p2"):
            @block.gpsimd
            def _(gpsimd):
                gpsimd.wait_ge(s_ve.h, s_ve.n)
                s_gp.inc(gpsimd.dma_start(out=dy[0:128, :],
                                          in_=ycat[:, 0:1024]), 16)
                gpsimd.wait_ge(s_gp.h, s_gp.n)
            scY.close()
            return nc

        out_acc = ctx.enter_context(
            nc.sbuf_tensor("out_acc", [128, NT * 1024], F32, side="right"))
        for ts in range(NT):
            for eh in range(2):
                pbk = "AB"[(2 * ts + eh) % 2]

                @block.tensor
                def _(tensor):
                    if ts == 0 and eh == 0:
                        tensor.wait_ge(s_ve.h, attn_done)
                        tensor.wait_ge(s_cw.h, n_const2)
                    t_guard(tensor, pbk)
                    for c in range(8):
                        s_mm.inc(tensor.matmul(
                            bank[pbk][:],
                            ycat[:, c * TOWN + ts * 128:c * TOWN + (ts + 1) * 128],
                            wproj[:, c * 1024 + eh * 512:c * 1024 + (eh + 1) * 512],
                            start=(c == 0), stop=(c == 7)))
                nm = s_mm.n

                @block.vector
                def _(vector):
                    vector.wait_ge(s_mm.h, nm)
                    dst = out_acc[:, ts * 1024 + eh * 512:ts * 1024 + (eh + 1) * 512]
                    v = vector.tensor_tensor(dst, bank[pbk][:],
                                             bproj[:, eh * 512:(eh + 1) * 512], Alu.add)
                    b_done(v, pbk)
                    s_ve.bump(vector)
        proj_done = s_ve.n
        scY.close()
        if stop_after == "p3":
            @block.sync
            def _(sync):
                sync.wait_ge(s_ve.h, s_ve.n)
                s_st.inc(sync.dma_start(out=dy[0:128, :],
                                        in_=out_acc[:, 0:1024]), 16)
                sync.wait_ge(s_st.h, s_st.n)
            return nc

        with ExitStack() as p4:
            def e4(name, shape, dt_):
                return p4.enter_context(nc.sbuf_tensor(name, shape, dt_, side="left"))

            s_fm = e4("s_fm", [128, 8 * TOWN], BF16)
            gbc = e4("gbc", [128, TOWN], BF16)
            grow = e4("grow", [1, TOWN], BF16)
            w1r = [e4(f"w1r{i}", [128, 8 * 256], BF16) for i in range(4)]
            w2r = [e4(f"w2r{i}", [128, 8 * 512], BF16) for i in range(2)]
            mA = [e4(f"mA{i}", [128, 512], F32) for i in range(2)]
            r2b = [e4(f"r2b{i}", [128, 512], F32) for i in range(2)]
            sgb = [e4(f"sgb{i}", [128, 512], BF16) for i in range(2)]
            b2c = [e4(f"b2c{i}", [128, 512], BF16) for i in range(2)]

            for ts in range(NT):
                for eh in range(2):
                    @block.tensor
                    def _(tensor):
                        if ts == 0 and eh == 0:
                            tensor.wait_ge(s_ve.h, proj_done)
                            tensor.wait_ge(s_cg.h, n_cg1)
                        t_guard(tensor, "Y0")
                        s_mm.inc(tensor.matmul(
                            pY0[:], gfm[:, ts * 128:(ts + 1) * 128],
                            b2bf[:, eh * 512:(eh + 1) * 512], start=True, stop=True))
                    nm = s_mm.n

                    @block.vector
                    def _(vector):
                        vector.wait_ge(s_mm.h, nm)
                        dst = out_acc[:, ts * 1024 + eh * 512:ts * 1024 + (eh + 1) * 512]
                        v = vector.tensor_tensor(dst, dst, pY0[:], Alu.add)
                        b_done(v, "Y0")
                        s_ve.bump(vector)
            b2g_done = s_ve.n

            w1_loads = [0]
            w2_loads = [0]
            w1_free = []
            w2_free = []

            w1_tgt = {}
            w2_tgt = {}

            def load_w1(e_, piece, slot):
                @block.gpsimd
                def _(gpsimd):
                    if w1_loads[0] >= 4:
                        gpsimd.wait_ge(s_ve.h, w1_free[w1_loads[0] - 4])
                    src = dw1[e_, piece].rearrange("(c p) n -> p c n", p=128)
                    s_w1r[slot].inc(gpsimd.dma_start(
                        out=w1r[slot][:].rearrange("p (c n) -> p c n", c=8),
                        in_=src), 16)
                w1_tgt[e_ * 8 + piece] = (slot, s_w1r[slot].n)
                w1_loads[0] += 1

            def load_w2(e_, half, slot):
                @block.gpsimd
                def _(gpsimd):
                    if w2_loads[0] >= 2:
                        gpsimd.wait_ge(s_ve.h, w2_free[w2_loads[0] - 2])
                    src = (dw2[e_].rearrange("(c p) n -> p c n", p=128)
                           [:, :, half * 512:(half + 1) * 512])
                    s_w2r[slot].inc(gpsimd.dma_start(
                        out=w2r[slot][:].rearrange("p (c n) -> p c n", c=8),
                        in_=src), 16)
                w2_tgt[e_ * 2 + half] = (slot, s_w2r[slot].n)
                w2_loads[0] += 1

            @block.gpsimd
            def _(gpsimd):
                gpsimd.wait_ge(s_ve.h, proj_done)

            for pc in range(4):
                load_w1(0, pc, pc)
            load_w2(0, 0, 0)
            load_w2(0, 1, 1)

            glu_sv = []
            w2_mm_done = [0] * NEXP

            gbc_mm_done = 0
            for e_ in range(NEXP):
                @block.gpsimd
                def _(gpsimd):
                    if e_ > 0:
                        gpsimd.wait_ge(s_mm.h, gbc_mm_done)
                    s_gp.inc(gpsimd.dma_start(out=grow[:],
                                              in_=gfm[e_:e_ + 1, :]), 16)
                ngrow = s_gp.n

                for g in range(4):
                    @block.tensor
                    def _(tensor):
                        if e_ == 0 and g == 0:
                            tensor.wait_ge(s_ve.h, b2g_done)
                        if g == 0:
                            tensor.wait_ge(s_gp.h, ngrow)
                        t_guard(tensor, "Y1")
                        s_mm.inc(tensor.matmul(
                            pY1[:], onesr[:],
                            grow[0:1, g * 512:(g + 1) * 512],
                            start=True, stop=True))
                    nm = s_mm.n

                    @block.vector
                    def _(vector):
                        vector.wait_ge(s_mm.h, nm)
                        v = vector.tensor_copy(gbc[:, g * 512:(g + 1) * 512], pY1[:])
                        b_done(v, "Y1")
                        s_ve.bump(vector)
                gbc_mm_done = s_mm.n

                for j in range(8):
                    slot, w1_target = w1_tgt[e_ * 8 + j]
                    for g in range(4):
                        step = len(glu_sv)
                        ba = "AC"[g % 2]
                        bb = "BD"[g % 2]
                        bi = g % 2

                        @block.tensor
                        def _(tensor):
                            if g == 0:
                                tensor.wait_ge(s_w1r[slot].h, w1_target)
                            t_guard(tensor, ba)
                            t_guard(tensor, bb)
                            for c in range(8):
                                lA = w1r[slot][:, c * 256:c * 256 + 128]
                                lB = w1r[slot][:, c * 256 + 128:c * 256 + 256]
                                rhs = xnffn3[:, c, g * 512:(g + 1) * 512]
                                s_mm.inc(tensor.matmul(bank[ba][:], lA, rhs,
                                                       start=(c == 0), stop=(c == 7)))
                                s_mm.inc(tensor.matmul(bank[bb][:], lB, rhs,
                                                       start=(c == 0), stop=(c == 7)))
                        nm = s_mm.n

                        @block.vector
                        def _(vector):
                            vector.wait_ge(s_mm.h, nm)
                            v = vector.tensor_scalar(
                                mA[bi][:], bank[ba][:],
                                b1c[:, e_ * 8 + j:e_ * 8 + j + 1], 7.0,
                                Alu.add, Alu.min)
                            b_done(v, ba)
                            s_ve.bump(vector)
                        nva = s_ve.n

                        @block.scalar
                        def _(scalar):
                            scalar.wait_ge(s_mm.h, nm)
                            if step >= 2:
                                scalar.wait_ge(s_ve.h, glu_sv[step - 2])
                            v = scalar.activation(
                                r2b[bi][:], bank[bb][:], Act.Relu,
                                bias=b1c7[:, e_ * 8 + j:e_ * 8 + j + 1])
                            b_done(v, bb)
                            scalar.wait_ge(s_ve.h, nva)
                            s_sc.inc(scalar.activation(sgb[bi][:], mA[bi][:],
                                                       Act.Gelu_apprx_sigmoid))
                        nsg = s_sc.n

                        @block.vector
                        def _(vector):
                            vector.wait_ge(s_sc.h, nsg)
                            if e_ > 0 and j == 0 and g == 0:
                                vector.wait_ge(s_mm.h, w2_mm_done[e_ - 1])
                            vector.tensor_scalar(b2c[bi][:], r2b[bi][:], 14.0, -6.0,
                                                 Alu.min, Alu.add)
                            vector.tensor_tensor(b2c[bi][:], sgb[bi][:], b2c[bi][:],
                                                 Alu.mult)
                            dst = s_fm[:, j * TOWN + g * 512:j * TOWN + (g + 1) * 512]
                            s_ve.inc(vector.tensor_tensor(
                                dst, b2c[bi][:], gbc[:, g * 512:(g + 1) * 512],
                                Alu.mult))
                        glu_sv.append(s_ve.n)

                    w1_free.append(s_ve.n)
                    nxt = e_ * 8 + j + 4
                    if nxt < NEXP * 8:
                        load_w1(nxt // 8, nxt % 8, slot)
                s_done = s_ve.n

                for eh in range(2):
                    w2slot, w2_target = w2_tgt[e_ * 2 + eh]
                    for ts in range(NT):
                        wb = ("AC" if eh == 0 else "BD")[ts % 2]

                        @block.tensor
                        def _(tensor):
                            if ts == 0:
                                tensor.wait_ge(s_w2r[w2slot].h, w2_target)
                                tensor.wait_ge(s_ve.h, s_done)
                            t_guard(tensor, wb)
                            for c in range(8):
                                s_mm.inc(tensor.matmul(
                                    bank[wb][:],
                                    s_fm[:, c * TOWN + ts * 128:c * TOWN + (ts + 1) * 128],
                                    w2r[w2slot][:, c * 512:(c + 1) * 512],
                                    start=(c == 0), stop=(c == 7)))
                        nm = s_mm.n

                        @block.vector
                        def _(vector):
                            vector.wait_ge(s_mm.h, nm)
                            dst = out_acc[:, ts * 1024 + eh * 512:
                                          ts * 1024 + (eh + 1) * 512]
                            v = vector.tensor_tensor(dst, dst, bank[wb][:], Alu.add)
                            b_done(v, wb)
                            s_ve.bump(vector)
                    w2_free.append(s_ve.n)
                    nxt2 = e_ * 2 + eh + 2
                    if nxt2 < NEXP * 2:
                        load_w2(nxt2 // 2, nxt2 % 2, eh)
                w2_mm_done[e_] = s_mm.n
        ffn_done = s_ve.n
        if stop_after == "p4":
            @block.sync
            def _(sync):
                sync.wait_ge(s_ve.h, s_ve.n)
                s_st.inc(sync.dma_start(out=dy[0:128, :],
                                        in_=out_acc[:, 0:1024]), 16)
                sync.wait_ge(s_st.h, s_st.n)
            return nc

        with ExitStack() as p5:
            def e5(name, shape, dt_):
                return p5.enter_context(nc.sbuf_tensor(name, shape, dt_, side="left"))

            xb5 = [e5(f"xb5{i}", [128, 1024], F32) for i in range(2)]
            tmp5 = [e5(f"tmp5{i}", [128, 1024], F32) for i in range(2)]
            lnr1 = e5("lnr1", [128, NT], F32)

            @block.vector
            def _(vector):
                vector.wait_ge(s_ve.h, ffn_done)
                vector.tensor_scalar(lnr1[:], lnr[:], 1.0, None, Alu.add)
                s_ve.bump(vector)
            nl = s_ve.n
            nvf_hist = []

            for ts in range(NT):
                sx5 = s_x5[ts % 2]

                @block.sync
                def _(sync):
                    if ts == 0:
                        sync.wait_ge(s_ve.h, ffn_done)
                    if ts >= 2:
                        sync.wait_ge(s_st5[ts % 2].h, s_st5[ts % 2].n)
                    sx5.inc(sync.dma_start(out=xb5[ts % 2][:],
                                           in_=dx[ts * 128:(ts + 1) * 128, :]), 16)
                nx = sx5.n

                @block.scalar
                def _(scalar):
                    scalar.wait_ge(sx5.h, nx)
                    scalar.wait_ge(s_ve.h, nl if ts < 2 else nvf_hist[ts - 2])
                    s_sc.inc(scalar.activation(tmp5[ts % 2][:], xb5[ts % 2][:],
                                               Act.Identity, scale=lnr1[:, ts:ts + 1],
                                               bias=lnm[:, ts:ts + 1]))
                nsx = s_sc.n

                @block.vector
                def _(vector):
                    vector.wait_ge(s_sc.h, nsx)
                    dst = out_acc[:, ts * 1024:(ts + 1) * 1024]
                    s_ve.inc(vector.tensor_tensor(dst, dst, tmp5[ts % 2][:], Alu.add))
                nvf = s_ve.n
                nvf_hist.append(nvf)

                @block.sync
                def _(sync):
                    sync.wait_ge(s_ve.h, nvf)
                    s_st5[ts % 2].inc(sync.dma_start(
                        out=dy[ts * 128:(ts + 1) * 128, :],
                        in_=out_acc[:, ts * 1024:(ts + 1) * 1024]), 16)

            @block.sync
            def _(sync):
                sync.wait_ge(s_st5[0].h, s_st5[0].n)
                sync.wait_ge(s_st5[1].h, s_st5[1].n)
                if s_st.n:
                    sync.wait_ge(s_st.h, s_st.n)

    return nc


def _host_prep(W_attn, b_attn, W_proj, b_proj, norm_scale, wg, bg, w1, b1, w2, b2):
    W = np.ascontiguousarray(np.asarray(W_attn, dtype=np.float32))
    Wq, Wk, Wv = W[:, 0:E], W[:, E:2 * E], W[:, 2 * E:3 * E]
    wqkv = np.empty((NPAIR, E, 384), dtype=np.float32)
    for p in range(NPAIR):
        h0, h1 = 2 * p, 2 * p + 1
        wqkv[p] = np.concatenate([
            Wq[:, h0 * DH:(h0 + 1) * DH], Wq[:, h1 * DH:(h1 + 1) * DH],
            Wk[:, h0 * DH:(h0 + 1) * DH], Wk[:, h1 * DH:(h1 + 1) * DH],
            Wv[:, h0 * DH:(h0 + 1) * DH], Wv[:, h1 * DH:(h1 + 1) * DH]], axis=1)
    b = np.asarray(b_attn, dtype=np.float32)
    bq, bk, bv = b[0:E], b[E:2 * E], b[2 * E:3 * E]
    bqkv = np.empty((128, 24), dtype=np.float32)
    for p in range(NPAIR):
        h0, h1 = 2 * p, 2 * p + 1
        for k_, src in [(0, bq), (1, bk), (2, bv)]:
            bqkv[:, 3 * p + k_] = np.concatenate(
                [src[h0 * DH:(h0 + 1) * DH], src[h1 * DH:(h1 + 1) * DH]])

    w1 = np.asarray(w1, dtype=np.float32)
    w1a, w1b = w1[:, :, 0::2], w1[:, :, 1::2]
    w1s = np.empty((NEXP, 8, E, 256), dtype=np.float32)
    for j in range(8):
        w1s[:, j, :, 0:128] = w1a[:, :, j * 128:(j + 1) * 128]
        w1s[:, j, :, 128:256] = w1b[:, :, j * 128:(j + 1) * 128]
    b1 = np.asarray(b1, dtype=np.float32)
    b1a, b1b = b1[:, 0::2], b1[:, 1::2]
    b1c = np.empty((128, 256), dtype=np.float32)
    for e_ in range(NEXP):
        for j in range(8):
            b1c[:, e_ * 8 + j] = b1a[e_, j * 128:(j + 1) * 128]
            b1c[:, 128 + e_ * 8 + j] = b1b[e_, j * 128:(j + 1) * 128]

    return dict(
        wqkv=wqkv, bqkv=bqkv,
        wproj=np.ascontiguousarray(np.asarray(W_proj, dtype=np.float32)),
        bproj=np.broadcast_to(np.asarray(b_proj, dtype=np.float32), (128, E)).copy(),
        wg=np.ascontiguousarray(np.asarray(wg, dtype=np.float32)),
        bg=np.broadcast_to(np.asarray(bg, dtype=np.float32), (128, NEXP)).copy(),
        w1s=w1s, b1c=b1c,
        w2=np.ascontiguousarray(np.asarray(w2, dtype=np.float32)),
        b2=np.ascontiguousarray(np.asarray(b2, dtype=np.float32)),
        nsc=np.asarray(norm_scale, dtype=np.float32).reshape(8, 128).T.copy(),
        idn=np.eye(128, dtype=BF),
        idnf=np.eye(128, dtype=np.float32),
        onesr=np.ones((1, 128), dtype=BF),
        onesf=np.ones((1, 128), dtype=np.float32),
    )


def _masks(par):
    k = np.arange(128)
    gk = k + par * (1 - 2 * (k % 2))
    gq = 2 * np.arange(128) + par
    m1 = (gk[:, None] <= gq[None, :]).astype(BF)
    m2 = ((gk[:, None] + 128) <= gq[None, :]).astype(BF)
    return m1, m2


_NC_CACHE = {}


def kernel(L0_attn_x, L0_attn_W_attn, L0_attn_b_attn, L0_attn_W_proj,
           L0_attn_b_proj, L0_ffn_norm_scale, L0_ffn_wg, L0_ffn_bg,
           L0_ffn_w1, L0_ffn_b1, L0_ffn_w2, L0_ffn_b2):
    x = np.asarray(L0_attn_x, dtype=np.float32)
    shared = _host_prep(L0_attn_W_attn, L0_attn_b_attn, L0_attn_W_proj,
                        L0_attn_b_proj, L0_ffn_norm_scale, L0_ffn_wg,
                        L0_ffn_bg, L0_ffn_w1, L0_ffn_b1, L0_ffn_w2, L0_ffn_b2)

    in_maps = []
    for c in range(8):
        b_, par = c // 2, c % 2
        xs = x[b_]
        xp = np.concatenate([xs[par::2], xs[1 - par::2]], axis=0)
        m1, m2 = _masks(par)
        im = dict(shared)
        im["x"] = np.ascontiguousarray(xp)
        im["mask1"] = m1
        im["mask2"] = m2
        in_maps.append(im)

    if "nc" not in _NC_CACHE:
        _NC_CACHE["nc"] = build_nc()
    nc = _NC_CACHE["nc"]
    res = run_bass_kernel_spmd(nc, in_maps, core_ids=list(range(8)))

    out = np.empty((B, S, E), dtype=np.float32)
    for c in range(8):
        b_, par = c // 2, c % 2
        out[b_, par::2, :] = res.results[c]["y"]
    return out
